# version 16
# speedup vs baseline: 1.0312x; 1.0048x over previous
import numpy as np
import concourse.bass as bass
import concourse.mybir as mybir
from concourse.alu_op_type import AluOpType
from concourse.bass_utils import run_bass_kernel_spmd

F32 = mybir.dt.float32
BF16 = mybir.dt.bfloat16
AF = mybir.ActivationFunctionType
AX = mybir.AxisListType

D = 1024
T = 4096
NC_ = 8
TC = 8
TT32 = 32
EPS = 1e-5

ENGINES = ("tensor", "vector", "scalar", "gpsimd", "sync")


class Op:
    __slots__ = ("idx", "engine", "emit", "reads", "writes", "is_dma",
                 "deps", "need_inc", "count", "tag", "_waits", "semkey")

    def __init__(self, idx, engine, emit, reads, writes, is_dma, tag):
        self.idx = idx
        self.engine = engine
        self.emit = emit
        self.reads = reads
        self.writes = writes
        self.is_dma = is_dma
        self.deps = set()
        self.need_inc = False
        self.count = None
        self.tag = tag


class Prog:
    def __init__(self, nc):
        self.nc = nc
        self.ops = []
        self.last_writer = {}
        self.readers = {}

    def add(self, engine, emit, reads=(), writes=(), is_dma=False, tag=""):
        op = Op(len(self.ops), engine, emit, list(reads), list(writes), is_dma, tag)
        deps = set()
        for r in op.reads:
            w = self.last_writer.get(r)
            if w is not None:
                deps.add(w)
        for w in op.writes:
            lw = self.last_writer.get(w)
            if lw is not None:
                deps.add(lw)
            for rd in self.readers.get(w, ()):
                deps.add(rd)
        deps.discard(op.idx)
        pruned = set()
        for d in deps:
            dop = self.ops[d]
            if (dop.engine == "tensor" and engine == "tensor"
                    and not dop.is_dma and not op.is_dma):
                continue
            pruned.add(d)
        op.deps = pruned
        for w in op.writes:
            self.last_writer[w] = op.idx
            self.readers[w] = []
        for r in op.reads:
            self.readers.setdefault(r, []).append(op.idx)
        self.ops.append(op)
        return op

    def finalize(self, dma_strict=True, dma_lanes=0):
        ops = self.ops
        for op in ops:
            for d in op.deps:
                ops[d].need_inc = True
        for op in ops:
            if op.is_dma:
                op.need_inc = True
        counters = {}
        qord = {}
        for op in ops:
            if op.is_dma:
                if dma_lanes:
                    j = qord.get(op.engine, 0)
                    qord[op.engine] = j + 1
                    key = ("dma", op.engine, j % dma_lanes)
                else:
                    key = ("dma", op.engine)
            else:
                key = op.engine
            op.semkey = key
            if op.need_inc:
                inc = 16 if op.is_dma else 1
                counters[key] = counters.get(key, 0) + inc
                op.count = counters[key]
        self.totals = dict(counters)
        self._qdma = {}
        if dma_strict and not dma_lanes:
            for op in ops:
                if op.is_dma:
                    self._qdma.setdefault(op.semkey, []).append(
                        (op.idx, op.count))
        knowledge = {e: {} for e in ENGINES}
        opknow = [None] * len(ops)
        for op in ops:
            e = op.engine
            kn = knowledge[e]
            waits = {}
            if op.is_dma and dma_lanes and op.count > 16:
                if kn.get(op.semkey, 0) < op.count - 16:
                    waits[op.semkey] = op.count - 16
            for d in sorted(op.deps):
                dop = ops[d]
                dkey = dop.semkey
                tgt = dop.count
                if dop.is_dma and not dma_lanes:
                    tgt = self._dma_wait_count(dkey, op.idx, dop.count)
                if kn.get(dkey, 0) < tgt and waits.get(dkey, 0) < tgt:
                    waits[dkey] = tgt
            for dkey, cnt in waits.items():
                kn[dkey] = max(kn.get(dkey, 0), cnt)
            for d in op.deps:
                dk = opknow[ops[d].idx]
                if dk:
                    for k2, c2 in dk.items():
                        if kn.get(k2, 0) < c2:
                            kn[k2] = c2
            op._waits = waits
            snap = dict(kn)
            if op.count is not None:
                snap[op.semkey] = max(snap.get(op.semkey, 0), op.count)
            opknow[op.idx] = snap
        self.stats = dict(
            n_ops=len(ops),
            n_waits=sum(len(op._waits) for op in ops),
            n_incs=sum(1 for op in ops if op.need_inc),
            per_engine={e: sum(1 for o in ops if o.engine == e) for e in ENGINES},
        )

    def _dma_wait_count(self, dkey, opidx, depcount):
        import bisect
        lst = self._qdma.get(dkey)
        if not lst:
            return depcount
        pos = bisect.bisect_left(lst, (opidx, -1))
        if pos == 0:
            return depcount
        return max(depcount, lst[pos - 1][1])

    def emit_engine(self, eng, engine_name, sem_of):
        for op in [o for o in self.ops if o.engine == engine_name]:
            for dkey, cnt in sorted(op._waits.items(), key=str):
                eng.wait_ge(sem_of[dkey], cnt)
            inst = op.emit(eng)
            if op.need_inc:
                assert inst is not None, f"op {op.tag} needs inc, emit returned None"
                inst.then_inc(sem_of[op.semkey], 16 if op.is_dma else 1)


DMA_STRICT = True
DMA_LANES = 8


def emit_all(nc, prog, final_engine="sync"):
    from contextlib import ExitStack
    prog.finalize(dma_strict=DMA_STRICT, dma_lanes=DMA_LANES)
    with ExitStack() as st:
        sem_of = {}
        for key in prog.totals:
            nm = "S_" + "_".join(str(p) for p in (key if isinstance(key, tuple) else (key,)))
            sem_of[key] = st.enter_context(nc.semaphore(nm))
        block = st.enter_context(nc.Block())
        for e in ENGINES:
            def section(eng, _e=e):
                prog.emit_engine(eng, _e, sem_of)
                if _e == final_engine:
                    for key, tot in sorted(prog.totals.items(), key=str):
                        if isinstance(key, tuple) and key[0] == "dma" and tot:
                            eng.wait_ge(sem_of[key], tot)
            getattr(block, e)(section)
    return prog.stats


class Rg:
    __slots__ = ("ap", "keys")

    def __init__(self, ap, keys):
        self.ap = ap
        self.keys = keys


class Builder:
    def __init__(self, dump=()):
        self.nc = bass.Bass("TRN2", target_bir_lowering=False)
        self.prog = Prog(self.nc)
        self.dump = set(dump)
        self.hbm = {}
        self.slabs = {}
        self.ps = []
        self.ctx = None

    def din(self, name, shape, dt=F32):
        ap = self.nc.dram_tensor(name, list(shape), dt, kind="ExternalInput").ap()
        self.hbm[name] = ap
        return ap

    def dscr(self, name, shape, dt=F32):
        kind = "ExternalOutput" if name in self.dump else "Internal"
        ap = self.nc.dram_tensor(name, list(shape), dt, kind=kind).ap()
        self.hbm[name] = ap
        return ap

    def dout(self, name, shape, dt=F32):
        ap = self.nc.dram_tensor(name, list(shape), dt, kind="ExternalOutput").ap()
        self.hbm[name] = ap
        return ap

    def mkslab(self, name, ncols_f32):
        t = self.ctx.enter_context(self.nc.sbuf_tensor(name, [128, ncols_f32], F32))
        self.slabs[name] = (t, ncols_f32)

    def r(self, slab, c0, ncols, dt=F32, p0=0, np_=128):
        t, tot = self.slabs[slab]
        assert c0 + ncols <= tot, (slab, c0, ncols, tot)
        if dt == F32:
            ap = t[p0:p0 + np_, c0:c0 + ncols]
        elif dt == BF16:
            tb = t.bitcast(BF16)
            ap = tb[p0:p0 + np_, 2 * c0:2 * c0 + 2 * ncols]
        else:
            raise ValueError(dt)
        keys = [(slab, i) for i in range(c0 // 512, (c0 + ncols + 511) // 512)]
        return Rg(ap, keys)

    def psum_banks(self):
        self.psd = []
        for i in range(4):
            d = self.ctx.enter_context(
                self.nc.psum_tensor(f"pdbl{i}", [128, 1024], F32))
            self.psd.append(d)
            self.ps.append(d[0:128, 0:512])
            self.ps.append(d[0:128, 512:1024])

    def add(self, engine, emit, reads=(), writes=(), is_dma=False, tag=""):
        def kx(lst):
            out = []
            for x in lst:
                if isinstance(x, Rg):
                    out.extend(x.keys)
                else:
                    out.append(x)
            return out
        rk = kx(reads)
        wk = kx(writes)
        for k in rk:
            if isinstance(k, tuple) and k and k[0] == "ps" and k not in wk:
                wk.append(k)
        return self.prog.add(engine, emit, rk, wk, is_dma, tag)

    def load(self, dst, src_ap, reads=(), q="sync", tag="ld"):
        dap = dst.ap if isinstance(dst, Rg) else dst
        self.add(q, lambda e: e.dma_start(out=dap, in_=src_ap),
                 reads=list(reads), writes=[dst], is_dma=True, tag=tag)

    def loadT(self, dst, src_ap, reads=(), q="sync", tag="ldT"):
        dap = dst.ap if isinstance(dst, Rg) else dst
        self.add(q, lambda e: e.dma_start_transpose(out=dap, in_=src_ap),
                 reads=list(reads), writes=[dst], is_dma=True, tag=tag)

    def store(self, dst_ap, src, extra_reads=(), q="gpsimd", tag="st", dstkey=None):
        sap = src.ap if isinstance(src, Rg) else src
        wr = [dstkey] if dstkey else []
        self.add(q, lambda e: e.dma_start(out=dst_ap, in_=sap),
                 reads=[src] + list(extra_reads), writes=wr, is_dma=True, tag=tag)


def host_prep(inputs):
    f32 = np.float32
    g = {}
    D_, T_ = 1024, 4096
    p0 = np.concatenate([np.arange(0, D_, 2), np.arange(1, D_, 2)])
    p1 = np.zeros(D_, dtype=np.int64)
    for h in range(16):
        for i in range(32):
            p1[32 * h + i] = 64 * h + 2 * i
            p1[512 + 32 * h + i] = 64 * h + 2 * i + 1
    C = np.ascontiguousarray

    freqs = np.exp(-np.arange(512, dtype=np.float64) * (np.log(10000.0) / 512)).astype(f32)
    th = np.arange(T_, dtype=f32)[:, None] * freqs[None, :].astype(f32)
    g["cosT"] = np.cos(th).astype(f32)
    g["sinT"] = np.sin(th).astype(f32)
    g["identf"] = np.eye(128, dtype=f32)
    g["ones1x128"] = np.ones((1, 128), dtype=f32)
    g["onescol"] = np.ones((128, 1), dtype=f32)
    mb = np.zeros((128, 4, 512), dtype=f32)
    for rl in range(4):
        jj = np.arange(128)[:, None] + 128 * rl
        ii = np.arange(512)[None, :]
        mb[:, rl, :] = np.where(jj <= ii, 0.0, -1e32)
    g["maskneg"] = C(mb.reshape(128, 4 * 512))

    def col8(b):
        return C(b.reshape(8, 128).T.astype(f32))

    def col32(b):
        return C(b.reshape(32, 128).T.astype(f32))

    I = inputs
    g["wq0"] = C(I["L0_attn_Wq"][:, p0])
    g["wk0"] = C(I["L0_attn_Wk"][:, p0])
    g["wv0"] = C(I["L0_attn_Wv"])
    g["wg0"] = C(I["L0_attn_gate_W"][:, p0])
    g["wp0"] = C(I["L0_attn_p_W"][:, p0])
    g["wck"] = col8(I["L0_attn_clock_W"][:, 0])
    g["ckb"] = C(I["L0_attn_clock_b"].reshape(1, 1))
    g["gb8"] = col8(I["L0_attn_gate_b"][p0])
    g["pb8"] = col8(I["L0_attn_p_b"][p0])
    g["ln1g8"] = col8(I["L0_attn_ln1_g"])
    g["ln1b8"] = col8(I["L0_attn_ln1_b"])
    g["ln2g8"] = col8(I["L0_attn_ln2_g"])
    g["ln2b8"] = col8(I["L0_attn_ln2_b"])
    g["outW"] = C(I["L0_attn_out_W"])
    g["outb8"] = col8(I["L0_attn_out_b"])
    g["mW1"] = C(I["L0_attn_mlp_W1"])
    g["mb1c"] = col32(I["L0_attn_mlp_b1"])
    g["mW2"] = C(I["L0_attn_mlp_W2"])
    g["mb2c"] = col8(I["L0_attn_mlp_b2"])
    g["wrT"] = C(I["L0_ffn_Wr"].T)
    sel8 = np.zeros((8, 8 * 128), dtype=f32)
    for e in range(8):
        sel8[e, e * 128:(e + 1) * 128] = 1.0
    g["sel8"] = sel8
    g["fW1"] = C(I["L0_ffn_W1"][7])
    g["fb1c"] = col32(I["L0_ffn_b1"][7])
    g["fW2"] = C(I["L0_ffn_W2"][7])
    g["fb2c"] = col8(I["L0_ffn_b2"][7])
    g["wq1T"] = C(I["L1_attn_Wq"][p1, :].T)
    g["wk1T"] = C(I["L1_attn_Wk"][p1, :].T)
    g["wv1T"] = C(I["L1_attn_Wv"].T)
    g["wuT"] = C(I["L1_attn_Wu"].T)
    g["bu8"] = col8(I["L1_attn_bu"])
    g["g1T"] = C(I["L1_ffn_gate_w"].T)
    g["ewT"] = C(I["L1_ffn_expert_w"].transpose(0, 2, 1))
    return g

def build_program(dump=(), upto=None):
    from contextlib import ExitStack
    b = Builder(dump=dump)
    nc = b.nc
    xT = b.din("xT", [D, T])
    for n, s in [("cosT", [T, 512]), ("sinT", [T, 512]), ("identf", [128, 128]),
                 ("ones1x128", [1, 128]), ("onescol", [128, 1]),
                 ("maskneg", [128, 4 * 512]),
                 ("wq0", [D, D]), ("wk0", [D, D]), ("wv0", [D, D]),
                 ("wg0", [D, D]), ("wp0", [D, D]), ("wck", [128, 8]), ("ckb", [1, 1]),
                 ("gb8", [128, 8]), ("pb8", [128, 8]),
                 ("ln1g8", [128, 8]), ("ln1b8", [128, 8]),
                 ("ln2g8", [128, 8]), ("ln2b8", [128, 8]),
                 ("outW", [D, D]), ("outb8", [128, 8]),
                 ("mW1", [D, 4 * D]), ("mb1c", [128, 32]),
                 ("mW2", [4 * D, D]), ("mb2c", [128, 8]),
                 ("wrT", [D, 8]), ("sel8", [8, 1024]), ("fW1", [D, 4 * D]), ("fb1c", [128, 32]),
                 ("fW2", [4 * D, D]), ("fb2c", [128, 8]),
                 ("wq1T", [D, D]), ("wk1T", [D, D]), ("wv1T", [D, D]),
                 ("wuT", [D, D]), ("bu8", [128, 8]), ("g1T", [D, 8]),
                 ("ewT", [8, D, D])]:
        b.din(n, s)
    b.dscr("xnf", [D, T]); b.dscr("m0row", [1, T]); b.dscr("ir0row", [1, T])
    b.dscr("ckrow", [1, T]); b.dscr("p7row", [1, T]); b.dscr("mrow", [8, T])
    b.dscr("qtm0", [T, D], BF16); b.dscr("ktm0", [T, D], BF16)
    b.dscr("vtm0", [T, D], BF16)
    b.dscr("qfm0", [D, T], BF16); b.dscr("kfm0", [D, T], BF16)
    b.dscr("attn0", [D, T], BF16)
    b.dscr("x1p", [D, T]); b.dscr("x1s", [D, T])
    b.dscr("znf", [D, T]); b.dscr("zbfh", [D, T], BF16)
    b.dscr("qtm1", [T, D], BF16); b.dscr("ktm1", [T, D], BF16)
    b.dscr("vtm1", [T, D], BF16)
    b.dscr("attn1t", [T, D], BF16)
    b.dscr("yp", [D, T])
    b.dscr("mW1b", [128, 32 * 1024], BF16)
    b.dscr("fW1b", [128, 32 * 1024], BF16)
    b.dscr("xnfb", [D, T], BF16)
    y = b.dout("y", [D, T])

    with ExitStack() as ctx:
        b.ctx = ctx
        b.mkslab("S1", 16384)
        b.mkslab("S2", 16640)
        b.mkslab("S3", 8192)
        b.mkslab("S4", 8192)
        b.mkslab("SM", 1024)
        b.psum_banks()
        _build_phases(b, y, upto)
        stats = emit_all(nc, b.prog, final_engine="sync")
    return b, stats


def _build_phases(b, y, upto):
    nc = b.nc
    H = b.hbm
    ps = b.ps
    A = b.add
    phases = []

    onescol_f = b.r("SM", 0, 1)
    ones1x128_f = Rg(b.slabs["SM"][0][0:1, 4:132], b.r("SM", 4, 128).keys)
    onescol_b = b.r("SM", 134, 1, BF16)
    identf_sb = b.r("SM", 256, 128)
    tr32 = Rg(b.slabs["SM"][0][0:32, 384:512], b.r("SM", 384, 128).keys)
    clockrow = Rg(b.slabs["S2"][0][0:1, 14336:16384],
                  b.r("S2", 14336, 2048).keys)
    clockrow = Rg(b.slabs["S2"][0][0:1, 12288:16384], b.r("S2", 12288, 4096).keys)
    b.load(onescol_f, H["onescol"][:, :], tag="ld_ones")
    b.load(identf_sb, H["identf"][:, :], tag="ld_identf")

    cpool = b.r("SM", 512, 32)
    cmap = {}

    def cst(v):
        if v not in cmap:
            i = len(cmap)
            cmap[v] = i
            ap = cpool.ap[:, i:i + 1]
            A("vector", lambda e, ap=ap, v=v: e.memset(ap, float(v)),
              writes=[("cst", v)], tag=f"cst_{v}")
        return cpool.ap[:, cmap[v]:cmap[v] + 1], ("cst", v)
    A("sync", lambda e: e.dma_start(out=ones1x128_f.ap, in_=H["ones1x128"][:, :]),
      writes=[ones1x128_f], is_dma=True, tag="ld_ones1")
    A("gpsimd", lambda e: e.dma_start(out=onescol_b.ap[:, 0:1], in_=H["onescol"][:, :]),
      writes=[onescol_b], is_dma=True, tag="ld_onesb")

    def bc_build(row_rg, dst_rg, dt=F32, tag="bc"):
        for i in range(8):
            bank = ps[i % 2]
            A("tensor", lambda e, i=i, bank=bank: e.matmul(
                bank[:, :], ones1x128_f.ap, row_rg.ap[0:1, i * 512:(i + 1) * 512],
                start=True, stop=True),
              reads=[row_rg, ones1x128_f], writes=[("ps", i % 2)], tag=f"{tag}_mm{i}")
            A("vector", lambda e, i=i, bank=bank: e.tensor_copy(
                dst_rg.ap[:, i * 512:(i + 1) * 512], bank[:, :]),
              reads=[("ps", i % 2)], writes=[dst_rg], tag=f"{tag}_ev{i}")

    def ph_ln(l):
        src = H["xT"] if l == 0 else H["x1s"]
        dstf = H["xnf"] if l == 0 else H["znf"]
        xb = [b.r("S3", 0, 4096), b.r("S3", 4096, 4096)]
        scr = [b.r("S4", 0, 2048 * 2), b.r("S4", 4096, 4096)]
        rA_ = Rg(b.slabs["S1"][0][0:1, 0:4096], b.r("S1", 0, 4096).keys)
        rB_ = Rg(b.slabs["S2"][0][0:1, 0:4096], b.r("S2", 0, 4096).keys)
        for c in range(8):
            xc = xb[c % 2]
            sc = scr[c % 2]
            b.load(xc, src[c * 128:(c + 1) * 128, :],
                   reads=([("x1s", ic2, c) for ic2 in range(8)] if l == 1 else []),
                   q=("sync" if c % 2 == 0 else "gpsimd"),
                   tag=f"L{l}ln_ldx0_{c}")
            A("scalar", lambda e, xc=xc, sc=sc: e.activation(
                out=sc.ap[:, 0:4096], in_=xc.ap, func=AF.Square),
              reads=[xc], writes=[sc], tag=f"L{l}ln_sq0_{c}")
            for ic in range(8):
                A("tensor", lambda e, c=c, ic=ic, xc=xc: e.matmul(
                    ps[ic][0:1, :], onescol_f.ap, xc.ap[:, ic * 512:(ic + 1) * 512],
                    start=(c == 0), stop=(c == 7)),
                  reads=[xc, onescol_f], writes=[("ps", ic)], tag=f"L{l}ln_s1_{c}_{ic}")
                A("tensor", lambda e, c=c, ic=ic, sc=sc: e.matmul(
                    ps[ic][32:33, :], onescol_f.ap, sc.ap[:, ic * 512:(ic + 1) * 512],
                    start=(c == 0), stop=(c == 7)),
                  reads=[sc, onescol_f], writes=[("ps", ic)], tag=f"L{l}ln_s2_{c}_{ic}")
        for ic in range(8):
            A("vector", lambda e, ic=ic: e.tensor_copy(
                rA_.ap[0:1, ic * 512:(ic + 1) * 512], ps[ic][0:1, :]),
              reads=[("ps", ic)], writes=[rA_], tag=f"L{l}ln_evs1_{ic}")
            A("vector", lambda e, ic=ic: e.tensor_copy(
                rB_.ap[0:1, ic * 512:(ic + 1) * 512], ps[ic][32:33, :]),
              reads=[("ps", ic)], writes=[rB_], tag=f"L{l}ln_evs2_{ic}")
        rA = rA_
        rB = rB_
        rC = Rg(b.slabs["S2"][0][0:1, 4096:8192], b.r("S2", 4096, 4096).keys)
        rD = Rg(b.slabs["S4"][0][0:1, 0:4096], b.r("S4", 0, 4096).keys)
        bc_r = b.r("S2", 8192, 4096)
        bc_mr = b.r("S2", 12288, 4096)
        A("vector", lambda e: e.tensor_scalar_mul(rC.ap, rA.ap, 1.0 / D),
          reads=[rA], writes=[rC], tag=f"L{l}ln_m")
        A("vector", lambda e: e.tensor_scalar_mul(rD.ap, rB.ap, 1.0 / D),
          reads=[rB], writes=[rD], tag=f"L{l}ln_ss")
        A("vector", lambda e: e.tensor_tensor(rA.ap, rC.ap, rC.ap, AluOpType.mult),
          reads=[rC], writes=[rA], tag=f"L{l}ln_m2")
        A("vector", lambda e: e.tensor_tensor(rD.ap, rD.ap, rA.ap, AluOpType.subtract),
          reads=[rD, rA], writes=[rD], tag=f"L{l}ln_v")
        epsap, epskey = cst(EPS)
        A("scalar", lambda e: e.activation(out=rA.ap, in_=rD.ap, func=AF.Sqrt, bias=epsap[0:1, 0:1]),
          reads=[rD, epskey], writes=[rA], tag=f"L{l}ln_sqv")
        A("vector", lambda e: e.reciprocal(rB.ap, rA.ap),
          reads=[rA], writes=[rB], tag=f"L{l}ln_r")
        A("vector", lambda e: e.tensor_tensor(rC.ap, rC.ap, rB.ap, AluOpType.mult),
          reads=[rC, rB], writes=[rC], tag=f"L{l}ln_mr")
        bc_build(rB, bc_r, tag=f"L{l}bc_r")
        bc_build(rC, bc_mr, tag=f"L{l}bc_mr")
        if l == 0:
            A("vector", lambda e: e.tensor_scalar(
                rB.ap, rD.ap, 1.0 + EPS, EPS + EPS * EPS, AluOpType.mult, AluOpType.add),
              reads=[rD], writes=[rB], tag="ln_z1")
            A("scalar", lambda e: e.activation(out=rC.ap, in_=rB.ap, func=AF.Sqrt),
              reads=[rB], writes=[rC], tag="ln_sq2")
            A("vector", lambda e: e.reciprocal(rB.ap, rC.ap),
              reads=[rC], writes=[rB], tag="ln_rec")
            A("vector", lambda e: e.tensor_tensor(rA.ap, rB.ap, rA.ap, AluOpType.mult),
              reads=[rB, rA], writes=[rA], tag="ln_s")
            bc_s = b.r("S4", 0, 2048, BF16)
            bc_build(rA, bc_s, tag="bc_s")
        g8 = b.r("SM", 740, 8)
        b8 = b.r("SM", 748, 8)
        if l == 0:
            b.load(g8, H["ln1g8"][:, :], tag="ld_ln1g")
            b.load(b8, H["ln1b8"][:, :], tag="ld_ln1b")
        scrU = Rg(b.slabs["S4"][0][:, 4096:8192], b.r("S4", 4096, 4096).keys)
        for c in range(8):
            xc = xb[c % 2]
            b.load(xc, src[c * 128:(c + 1) * 128, :],
                   reads=([("x1s", ic2, c) for ic2 in range(8)] if l == 1 else []),
                   q=("sync" if c % 2 == 0 else "gpsimd"),
                   tag=f"L{l}ln_ldx2_{c}")
            A("vector", lambda e, xc=xc: e.tensor_tensor(
                xc.ap, xc.ap, bc_r.ap, AluOpType.mult),
              reads=[xc, bc_r], writes=[xc], tag=f"L{l}ln_xr{c}")
            A("vector", lambda e, xc=xc: e.tensor_tensor(
                xc.ap, xc.ap, bc_mr.ap, AluOpType.subtract),
              reads=[xc, bc_mr], writes=[xc], tag=f"L{l}ln_xn{c}")
            b.store(H[("xnf" if l == 0 else "znf")][c * 128:(c + 1) * 128, :], xc,
                    tag=f"L{l}st_xnf{c}", dstkey=(("xnf" if l == 0 else "znf"), c))
            if l == 0:
                A("gpsimd", lambda e, xc=xc, c=c: e.dma_start(
                    out=H["xnfb"][c * 128:(c + 1) * 128, :], in_=xc.ap),
                  reads=[xc], writes=[("xnfb", c)], is_dma=True,
                  tag=f"st_xnfb{c}")
            xn1blk = b.r("S1", c * 2048, 2048, BF16)
            if l == 0:
                A("vector", lambda e, xc=xc: e.tensor_tensor(
                    scrU.ap, xc.ap, bc_s.ap, AluOpType.mult),
                  reads=[xc, bc_s], writes=[scrU], tag=f"ln_u{c}")
                A("vector", lambda e, c=c, blk=xn1blk: e.tensor_scalar(
                    blk.ap, scrU.ap, g8.ap[:, c:c + 1], b8.ap[:, c:c + 1],
                    AluOpType.mult, AluOpType.add),
                  reads=[scrU, g8, b8], writes=[xn1blk], tag=f"ln_xn1_{c}")
            else:
                A("vector", lambda e, blk=xn1blk, xc=xc: e.tensor_copy(blk.ap, xc.ap),
                  reads=[xc], writes=[xn1blk], tag=f"zb_{c}")
                b.store(H["zbfh"][c * 128:(c + 1) * 128, :], xn1blk,
                        tag=f"st_zbf{c}", dstkey=("zbfh", c))
    phases.append(("ln0", lambda: ph_ln(0)))

    def ph_qkv(l):
        wq = b.r("S2", 0, 4096, BF16)
        wk = b.r("S2", 4096, 4096, BF16)
        wv = b.r("S2", 0, 4096, BF16)
        wqn = "wq0" if l == 0 else "wq1T"
        wkn = "wk0" if l == 0 else "wk1T"
        wvn = "wv0" if l == 0 else "wv1T"
        if l == 0:
            wck_sb = b.r("SM", 544, 4, BF16)
            A("gpsimd", lambda e: e.dma_start(out=wck_sb.ap, in_=H["wck"][:, :]),
              writes=[wck_sb], is_dma=True, tag="ld_wck")
            ckb_sb = b.r("SM", 550, 1)
            A("sync", lambda e: e.dma_start(out=ckb_sb.ap[0:1, 0:1], in_=H["ckb"][:, :]),
              writes=[ckb_sb], is_dma=True, tag="ld_ckb")
            cb128 = b.r("SM", 552, 1)
            clock_col = b.r("SM", 560, 32)
        for nm, rg, qq in [(wqn, wq, "gpsimd"), (wkn, wk, "gpsimd")]:
            A(qq, lambda e, nm=nm, rg=rg: e.dma_start(
                out=rg.ap.rearrange("p (c f) -> p c f", c=8),
                in_=H[nm].rearrange("(c p) f -> p c f", p=128)),
              writes=[rg], is_dma=True, tag=f"ld_{nm}")
        cosb = [b.r("S3", 0, 512), b.r("S3", 512, 512)]
        sinb = [b.r("S3", 1024, 512), b.r("S3", 1536, 512)]
        qraw = b.r("S3", 2048, 1024)
        kraw = b.r("S3", 3072, 1024)
        scrA = b.r("S3", 4096, 512)
        scrB = b.r("S3", 4608, 512)
        scrC = b.r("S3", 7168, 512)
        scrD = b.r("S3", 7680, 512)
        qob = b.r("S3", 5120, 512, BF16)
        kob = b.r("S3", 5632, 512, BF16)
        qs_col = b.r("SM", 720, 4)
        ks_col = b.r("SM", 724, 4)
        sqscr = b.r("S3", 6144, 1024)
        qscl = b.r("SM", 730, 1)
        kscl = b.r("SM", 732, 1)

        def proj_tm(tt, w_rg, banks, tag):
            for hf in range(2):
                for c in range(8):
                    wcol = c * 1024 + hf * 512
                    A("tensor", lambda e, c=c, tt=tt, w_rg=w_rg, bk=banks[hf], wcol=wcol:
                      e.matmul(ps[bk][:, :],
                               b.r("S1", c * 2048, 2048, BF16).ap[:, tt * 128:(tt + 1) * 128],
                               w_rg.ap[:, wcol:wcol + 512],
                               start=(c == 0), stop=(c == 7)),
                      reads=[b.r("S1", c * 2048, 2048, BF16), w_rg],
                      writes=[("ps", banks[hf])], tag=f"{tag}_mm{tt}_{hf}_{c}")

        def rope_evac(tt, banks, raw, scol, scl, ob, dstname, do_l2, tag, scrA=None, scrB=None):
            for hf in range(2):
                A("vector", lambda e, hf=hf, raw=raw, bk=banks[hf]: e.tensor_copy(
                    raw.ap[:, hf * 512:(hf + 1) * 512], ps[bk][:, :]),
                  reads=[("ps", banks[hf])], writes=[raw], tag=f"{tag}_ev{tt}_{hf}")
            if do_l2:
                for hf in range(2):
                    A("scalar", lambda e, hf=hf, raw=raw, scol=scol: e.activation(
                        out=sqscr.ap[:, hf * 512:(hf + 1) * 512],
                        in_=raw.ap[:, hf * 512:(hf + 1) * 512], func=AF.Square,
                        accum_out=scol.ap[:, hf:hf + 1]),
                      reads=[raw], writes=[sqscr, scol], tag=f"{tag}_sq{tt}_{hf}")
                A("vector", lambda e, scol=scol: e.tensor_tensor(
                    scol.ap[:, 2:3], scol.ap[:, 0:1], scol.ap[:, 1:2], AluOpType.add),
                  reads=[scol], writes=[scol], tag=f"{tag}_ss{tt}")
                A("scalar", lambda e, scol=scol: e.activation(
                    out=scol.ap[:, 3:4], in_=scol.ap[:, 2:3], func=AF.Sqrt),
                  reads=[scol], writes=[scol], tag=f"{tag}_sqrt{tt}")
                A("vector", lambda e, scol=scol: e.tensor_scalar_max(
                    scol.ap[:, 0:1], scol.ap[:, 3:4], 1e-12),
                  reads=[scol], writes=[scol], tag=f"{tag}_mx{tt}")
                A("vector", lambda e, scol=scol, scl=scl: e.reciprocal(
                    scl.ap, scol.ap[:, 0:1]),
                  reads=[scol], writes=[scl], tag=f"{tag}_rc{tt}")
            cb, sb_ = cosb[tt % 2], sinb[tt % 2]
            q1 = raw.ap[:, 0:512]
            q2 = raw.ap[:, 512:1024]
            A("vector", lambda e, cb=cb: e.tensor_tensor(scrA.ap, q1, cb.ap, AluOpType.mult),
              reads=[raw, cb], writes=[scrA], tag=f"{tag}_e1_{tt}")
            A("vector", lambda e, sb_=sb_: e.tensor_tensor(scrB.ap, q2, sb_.ap, AluOpType.mult),
              reads=[raw, sb_], writes=[scrB], tag=f"{tag}_e2_{tt}")
            A("vector", lambda e: e.tensor_tensor(scrA.ap, scrA.ap, scrB.ap, AluOpType.subtract),
              reads=[scrA, scrB], writes=[scrA], tag=f"{tag}_e3_{tt}")
            if l == 0:
                A("scalar", lambda e, scl=scl, ob=ob: e.activation(
                    out=ob.ap[:, 0:512], in_=scrA.ap, func=AF.Copy,
                    scale=scl.ap[:, 0:1]),
                  reads=[scrA, scl], writes=[ob], tag=f"{tag}_oe_{tt}")
            else:
                oev = ob.ap.rearrange("p (h two i) -> p h two i", h=16, two=2, i=32)
                A("vector", lambda e, ob=ob, oev=oev: e.tensor_copy(
                    oev[:, :, 0, :], scrA.ap.rearrange("p (h i) -> p h i", h=16)),
                  reads=[scrA], writes=[ob], tag=f"{tag}_oe_{tt}")
            A("vector", lambda e, sb_=sb_: e.tensor_tensor(scrA.ap, q1, sb_.ap, AluOpType.mult),
              reads=[raw, sb_], writes=[scrA], tag=f"{tag}_o1_{tt}")
            A("vector", lambda e, cb=cb: e.tensor_tensor(scrB.ap, q2, cb.ap, AluOpType.mult),
              reads=[raw, cb], writes=[scrB], tag=f"{tag}_o2_{tt}")
            A("vector", lambda e: e.tensor_tensor(scrA.ap, scrA.ap, scrB.ap, AluOpType.add),
              reads=[scrA, scrB], writes=[scrA], tag=f"{tag}_o3_{tt}")
            if l == 0:
                A("scalar", lambda e, scl=scl, ob=ob: e.activation(
                    out=ob.ap[:, 512:1024], in_=scrA.ap, func=AF.Copy,
                    scale=scl.ap[:, 0:1]),
                  reads=[scrA, scl], writes=[ob], tag=f"{tag}_oo_{tt}")
            else:
                oev = ob.ap.rearrange("p (h two i) -> p h two i", h=16, two=2, i=32)
                A("vector", lambda e, ob=ob, oev=oev: e.tensor_copy(
                    oev[:, :, 1, :], scrA.ap.rearrange("p (h i) -> p h i", h=16)),
                  reads=[scrA], writes=[ob], tag=f"{tag}_oo_{tt}")
            b.store(H[dstname][tt * 128:(tt + 1) * 128, :], ob,
                    tag=f"{tag}_st{tt}", dstkey=(dstname, tt))

        qtmn = "qtm0" if l == 0 else "qtm1"
        ktmn = "ktm0" if l == 0 else "ktm1"
        vtmn = "vtm0" if l == 0 else "vtm1"
        for tt in range(32):
            b.load(cosb[tt % 2], H["cosT"][tt * 128:(tt + 1) * 128, :], tag=f"ldcos{tt}")
            b.load(sinb[tt % 2], H["sinT"][tt * 128:(tt + 1) * 128, :],
                   q="gpsimd", tag=f"ldsin{tt}")
            qbk = (0, 1) if tt % 2 == 0 else (4, 5)
            kbk = (2, 3) if tt % 2 == 0 else (6, 7)
            proj_tm(tt, wq, qbk, f"L{l}q")
            proj_tm(tt, wk, kbk, f"L{l}k")
            rope_evac(tt, qbk, qraw, qs_col, qscl, qob, qtmn, l == 0, f"L{l}q", scrA, scrB)
            rope_evac(tt, kbk, kraw, ks_col, kscl, kob, ktmn, l == 0, f"L{l}k", scrC, scrD)
        A("gpsimd", lambda e: e.dma_start(
            out=wv.ap.rearrange("p (c f) -> p c f", c=8),
            in_=H[wvn].rearrange("(c p) f -> p c f", p=128)),
          writes=[wv], is_dma=True, tag=f"ld_{wvn}")
        for tt in range(32):
            vbk = (0, 1) if tt % 2 == 0 else (2, 3)
            proj_tm(tt, wv, vbk, f"L{l}v")
            if l == 0:
                ckb = 6 + (tt % 2)
                for c in range(8):
                    A("tensor", lambda e, c=c, tt=tt, ckb=ckb: e.matmul(
                        ps[ckb][:, 0:1],
                        b.r("S1", c * 2048, 2048, BF16).ap[:, tt * 128:(tt + 1) * 128],
                        wck_sb.ap[:, c:c + 1], start=(c == 0), stop=(c == 7)),
                      reads=[b.r("S1", c * 2048, 2048, BF16), wck_sb],
                      writes=[("ps", ckb)], tag=f"ck_mm{tt}_{c}")
                A("vector", lambda e, tt=tt, ckb=ckb: e.tensor_copy(
                    clock_col.ap[:, tt:tt + 1], ps[ckb][:, 0:1]),
                  reads=[("ps", ckb)], writes=[clock_col], tag=f"ck_ev{tt}")
            vtile = b.r("S3", 6144 + (tt % 2) * 512, 512, BF16)
            for hf in range(2):
                A("vector", lambda e, hf=hf, vtile=vtile, bk=vbk[hf]: e.tensor_copy(
                    vtile.ap[:, hf * 512:(hf + 1) * 512], ps[bk][:, :]),
                  reads=[("ps", vbk[hf])], writes=[vtile], tag=f"L{l}v_ev{tt}_{hf}")
            b.store(H[vtmn][tt * 128:(tt + 1) * 128, :], vtile,
                    tag=f"L{l}v_st{tt}", dstkey=(vtmn, tt))
        if l == 0:
            A("tensor", lambda e: e.matmul(ps[7][:, 0:1], ones1x128_f.ap,
                                           ckb_sb.ap[0:1, 0:1], start=True, stop=True),
              reads=[ckb_sb, ones1x128_f], writes=[("ps", 7)], tag="bc_ckb")
            A("vector", lambda e: e.tensor_copy(cb128.ap, ps[7][:, 0:1]),
              reads=[("ps", 7)], writes=[cb128], tag="ev_ckb")
            c2 = b.r("SM", 592, 32)
            c3 = b.r("SM", 624, 32)
            c4 = b.r("SM", 656, 32)
            c5 = b.r("SM", 688, 32)
            A("vector", lambda e: e.tensor_scalar(
                c2.ap, clock_col.ap, cb128.ap[:, 0:1], None, AluOpType.add),
              reads=[clock_col, cb128], writes=[c2], tag="ck_addb")
            A("vector", lambda e: e.tensor_scalar_max(c3.ap, c2.ap, 0.0),
              reads=[c2], writes=[c3], tag="ck_mx")
            A("scalar", lambda e: e.activation(out=c4.ap, in_=c2.ap, func=AF.Abs),
              reads=[c2], writes=[c4], tag="ck_abs")
            A("scalar", lambda e: e.activation(out=c5.ap, in_=c4.ap, func=AF.Exp, scale=-1.0),
              reads=[c4], writes=[c5], tag="ck_exp")
            one_ap, one_key = cst(1.0)
            A("scalar", lambda e: e.activation(out=c4.ap, in_=c5.ap, func=AF.Ln, bias=one_ap),
              reads=[c5, one_key], writes=[c4], tag="ck_ln")
            A("vector", lambda e: e.tensor_tensor(c2.ap, c3.ap, c4.ap, AluOpType.add),
              reads=[c3, c4], writes=[c2], tag="ck_sp")
            A("tensor", lambda e: e.transpose(ps[7][0:32, 0:128], c2.ap, identf_sb.ap),
              reads=[c2, identf_sb], writes=[("ps", 7)], tag="ck_tr")
            A("vector", lambda e: e.tensor_copy(tr32.ap, ps[7][0:32, 0:128]),
              reads=[("ps", 7)], writes=[tr32], tag="ck_trev")
            A("gpsimd", lambda e: e.dma_start(
                out=H["ckrow"].rearrange("one (tile off) -> (one tile) off", tile=32),
                in_=tr32.ap),
              reads=[tr32], writes=[("ckrow", 0)], is_dma=True, tag="ck_row")
            b.load(clockrow, H["ckrow"][:, :], reads=[("ckrow", 0)], tag="ck_rowld")
    phases.append(("qkv0", lambda: ph_qkv(0)))

    def ph_gate():
        import os
        GPARTS = int(os.environ.get("K_GATE_PARTS", "99"))
        NCHUNK = int(os.environ.get("K_GATE_CHUNKS", "8"))
        wg = b.r("S2", 0, 4096, BF16)
        wp = b.r("S2", 4096, 4096, BF16)
        for nm, rg, qq in [("wg0", wg, "gpsimd"), ("wp0", wp, "gpsimd")]:
            A(qq, lambda e, nm=nm, rg=rg: e.dma_start(
                out=rg.ap.rearrange("p (c f) -> p c f", c=8),
                in_=H[nm].rearrange("(c p) f -> p c f", p=128)),
              writes=[rg], is_dma=True, tag=f"ld_{nm}")
        bc_clock = b.r("S2", 8192, 2048, BF16)
        bc_build(clockrow, bc_clock, tag="bc_ck")
        ones_sc = b.r("S2", 10240, 2048, BF16)
        A("vector", lambda e: e.memset(ones_sc.ap, 1.0), writes=[ones_sc], tag="ones_sc")
        gb8 = b.r("SM", 756, 8)
        pb8 = b.r("SM", 764, 8)
        b.load(gb8, H["gb8"][:, :], tag="ld_gb8")
        b.load(pb8, H["pb8"][:, :], tag="ld_pb8")
        F1 = b.r("S3", 0, 4096)
        F2 = b.r("S3", 4096, 4096)
        F3 = b.r("S4", 0, 4096)
        F4 = b.r("S4", 4096, 4096)
        sAs = [b.r("S2", 12288, 512), b.r("S2", 13312, 512)]
        sBs = [b.r("S2", 12800, 512), b.r("S2", 13824, 512)]
        qld = b.r("S2", 14336, 2048, BF16)
        pmx = b.r("SM", 772, 1)
        npmx = b.r("SM", 774, 1)
        one_ap, one_key = cst(1.0)
        eps8, eps8k = cst(1e-8)

        def proj_fm(w_rg, banks, dst, biasfn, tag, c):
            for ic in range(8):
                bank = banks[ic % len(banks)]
                for kc in range(8):
                    A("tensor", lambda e, ic=ic, kc=kc, bank=bank, w_rg=w_rg, c=c:
                      e.matmul(ps[bank][:, :],
                               w_rg.ap[:, kc * 1024 + c * 128: kc * 1024 + c * 128 + 128],
                               b.r("S1", kc * 2048, 2048, BF16).ap[:, ic * 512:(ic + 1) * 512],
                               start=(kc == 0), stop=(kc == 7)),
                      reads=[w_rg, b.r("S1", kc * 2048, 2048, BF16)],
                      writes=[("ps", bank)], tag=f"{tag}_mm{c}_{ic}_{kc}")
                biasfn(ic, bank, dst)

        for c in range(NCHUNK):
            def gate_evac(ic, bank, dst, c=c):
                sA = sAs[ic % 2]
                sB = sBs[ic % 2]
                A("scalar", lambda e, ic=ic, bank=bank, sA=sA: e.activation(
                    out=sA.ap, in_=ps[bank][:, :], func=AF.Abs, bias=gb8.ap[:, c:c + 1]),
                  reads=[("ps", bank), gb8], writes=[sA], tag=f"ga_{c}_{ic}")
                A("scalar", lambda e, sA=sA, sB=sB: e.activation(
                    out=sB.ap, in_=sA.ap, func=AF.Exp, scale=-1.0),
                  reads=[sA], writes=[sB], tag=f"ge_{c}_{ic}")
                A("scalar", lambda e, sA=sA, sB=sB: e.activation(
                    out=sA.ap, in_=sB.ap, func=AF.Ln, bias=one_ap),
                  reads=[sB, one_key], writes=[sA], tag=f"gl_{c}_{ic}")
                A("vector", lambda e, ic=ic, bank=bank: e.tensor_scalar(
                    dst.ap[:, ic * 512:(ic + 1) * 512], ps[bank][:, :],
                    gb8.ap[:, c:c + 1], 0.0, AluOpType.add, AluOpType.max),
                  reads=[("ps", bank), gb8], writes=[dst], tag=f"gm_{c}_{ic}")
                A("vector", lambda e, ic=ic, sA=sA: e.tensor_tensor(
                    dst.ap[:, ic * 512:(ic + 1) * 512],
                    dst.ap[:, ic * 512:(ic + 1) * 512], sA.ap, AluOpType.add),
                  reads=[dst, sA], writes=[dst], tag=f"gs_{c}_{ic}")
            proj_fm(wg, (0, 1, 4, 5), F1, gate_evac, "gproj", c)
            if GPARTS < 2:
                continue
            A("vector", lambda e: e.tensor_tensor_scan(
                F2.ap, ones_sc.ap, F1.ap, 0.0, AluOpType.mult, AluOpType.add),
              reads=[ones_sc, F1], writes=[F2], tag=f"gscan_{c}")
            A("vector", lambda e: e.tensor_scalar_min(F2.ap, F2.ap, 60.0),
              reads=[F2], writes=[F2], tag=f"gmin_{c}")
            A("scalar", lambda e: e.activation(out=F1.ap, in_=F2.ap, func=AF.Exp, scale=-1.0),
              reads=[F2], writes=[F1], tag=f"gG_{c}")
            if GPARTS < 3:
                continue
            def p_evac(ic, bank, dst, c=c):
                A("vector", lambda e, ic=ic, bank=bank, c=c: e.tensor_scalar(
                    dst.ap[:, ic * 512:(ic + 1) * 512], ps[bank][:, :],
                    pb8.ap[:, c:c + 1], None, AluOpType.add),
                  reads=[("ps", bank), pb8], writes=[dst], tag=f"pe_{c}_{ic}")
            proj_fm(wp, (2, 3, 6, 7), F2, p_evac, "pproj", c)
            A("vector", lambda e: e.reduce_max(pmx.ap, F2.ap, axis=AX.X),
              reads=[F2], writes=[pmx], tag=f"pmax_{c}")
            A("vector", lambda e: e.tensor_scalar_mul(npmx.ap, pmx.ap, -1.0),
              reads=[pmx], writes=[npmx], tag=f"npmax_{c}")
            A("scalar", lambda e: e.activation(
                out=F3.ap, in_=F2.ap, func=AF.Exp, bias=npmx.ap[:, 0:1]),
              reads=[F2, npmx], writes=[F3], tag=f"pexp_{c}")
            A("vector", lambda e: e.tensor_tensor(F3.ap, F3.ap, bc_clock.ap, AluOpType.mult),
              reads=[F3, bc_clock], writes=[F3], tag=f"pck_{c}")
            A("vector", lambda e: e.tensor_tensor_scan(
                F4.ap, ones_sc.ap, F3.ap, 0.0, AluOpType.mult, AluOpType.add),
              reads=[ones_sc, F3], writes=[F4], tag=f"pscan_{c}")
            if GPARTS < 4:
                continue
            A("scalar", lambda e: e.activation(out=F2.ap, in_=F4.ap, func=AF.Ln, bias=eps8),
              reads=[F4, eps8k], writes=[F2], tag=f"plog_{c}")
            A("scalar", lambda e: e.activation(out=F4.ap, in_=F2.ap, func=AF.Exp, scale=-1.0),
              reads=[F2], writes=[F4], tag=f"prec_{c}")
            A("vector", lambda e: e.tensor_tensor(F2.ap, F1.ap, F4.ap, AluOpType.mult),
              reads=[F1, F4], writes=[F2], tag=f"fq_{c}")
            if GPARTS < 5:
                continue
            b.loadT(qld, H["qtm0"][:, c * 128:(c + 1) * 128],
                    reads=[("qtm0", tt) for tt in range(32)], tag=f"qldT_{c}")
            A("vector", lambda e: e.tensor_tensor(qld.ap, qld.ap, F2.ap, AluOpType.mult),
              reads=[qld, F2], writes=[qld], tag=f"qfin_{c}")
            b.store(H["qfm0"][c * 128:(c + 1) * 128, :], qld,
                    tag=f"stq_{c}", dstkey=("qfm0", c))
            A("scalar", lambda e: e.activation(out=F4.ap, in_=F1.ap, func=AF.Ln, bias=eps8),
              reads=[F1, eps8k], writes=[F4], tag=f"glog_{c}")
            A("scalar", lambda e: e.activation(out=F1.ap, in_=F4.ap, func=AF.Exp, scale=-1.0),
              reads=[F4], writes=[F1], tag=f"grec_{c}")
            A("vector", lambda e: e.tensor_tensor(F3.ap, F3.ap, F1.ap, AluOpType.mult),
              reads=[F3, F1], writes=[F3], tag=f"fk_{c}")
            b.loadT(qld, H["ktm0"][:, c * 128:(c + 1) * 128],
                    reads=[("ktm0", tt) for tt in range(32)], tag=f"kldT_{c}")
            A("vector", lambda e: e.tensor_tensor(qld.ap, qld.ap, F3.ap, AluOpType.mult),
              reads=[qld, F3], writes=[qld], tag=f"kfin_{c}")
            b.store(H["kfm0"][c * 128:(c + 1) * 128, :], qld,
                    tag=f"stk_{c}", dstkey=("kfm0", c))
    phases.append(("gate0", ph_gate))

    def ph_attn0():
        for c in range(8):
            blk = b.r("S1", c * 2048, 2048, BF16)
            b.load(blk, H["kfm0"][c * 128:(c + 1) * 128, :],
                   reads=[("kfm0", c)], tag=f"at0_ldk{c}")
        maskneg = b.r("S4", 4096, 2048)
        b.load(maskneg, H["maskneg"][:, :], tag="at0_ldmask")
        qsl = [b.r("S4", 0, 2048, BF16), b.r("S4", 2048, 2048, BF16)]
        sA = b.r("S4", 6144, 512)
        bcr = b.r("S4", 6656, 512)
        stg = b.r("S4", 7168, 256, BF16)
        rcp = Rg(b.slabs["S4"][0][0:1, 7424:7936], b.r("S4", 7424, 512).keys)
        vload = {}

        for ic in range(8):
            qb = qsl[ic % 2]
            A("sync", lambda e, ic=ic, qb=qb: e.dma_start(
                out=qb.ap.rearrange("p (c n) -> p c n", c=8),
                in_=H["qfm0"][:, ic * 512:(ic + 1) * 512].rearrange(
                    "(c p) n -> p c n", p=128)),
              reads=[("qfm0", c) for c in range(8)], writes=[qb],
              is_dma=True, tag=f"at0_ldq{ic}")
            for jt in range(4 * ic, 4 * ic + 4):
                vt = b.r("S2", jt * 512, 512, BF16)
                vload[jt] = vt
                A("sync", lambda e, jt=jt, vt=vt: e.dma_start(
                    out=vt.ap, in_=H["vtm0"][jt * 128:(jt + 1) * 128, :]),
                  reads=[("vtm0", jt)], writes=[vt], is_dma=True, tag=f"at0_ldv{jt}")
            njt = 4 * ic + 4
            for jt in range(njt):
                bank = 5 + (jt % 3)
                for kc in range(8):
                    A("tensor", lambda e, jt=jt, kc=kc, bank=bank, ic=ic:
                      e.matmul(ps[bank][:, :],
                               b.r("S1", kc * 2048, 2048, BF16).ap[:, jt * 128:(jt + 1) * 128],
                               qsl[ic % 2].ap[:, kc * 512:(kc + 1) * 512],
                               start=(kc == 0), stop=(kc == 7)),
                      reads=[b.r("S1", kc * 2048, 2048, BF16), qsl[ic % 2]],
                      writes=[("ps", bank)], tag=f"at0_s{ic}_{jt}_{kc}")
                pt = b.r("S3", jt * 256, 256, BF16)
                if jt >= 4 * ic:
                    rel = jt - 4 * ic
                    A("vector", lambda e, bank=bank, rel=rel: e.scalar_tensor_tensor(
                        sA.ap, ps[bank][:, :], 1e30,
                        maskneg.ap[:, rel * 512:(rel + 1) * 512],
                        AluOpType.min, AluOpType.add),
                      reads=[("ps", bank), maskneg], writes=[sA], tag=f"at0_m{ic}_{jt}")
                    A("scalar", lambda e, pt=pt: e.activation(
                        out=pt.ap, in_=sA.ap, func=AF.Exp, scale=1.0 / 32.0),
                      reads=[sA], writes=[pt], tag=f"at0_e{ic}_{jt}")
                else:
                    A("scalar", lambda e, bank=bank, pt=pt: e.activation(
                        out=pt.ap, in_=ps[bank][:, :], func=AF.Exp, scale=1.0 / 32.0),
                      reads=[("ps", bank)], writes=[pt], tag=f"at0_e{ic}_{jt}")
            for half in range(2):
                dcs = (0, 1, 2, 3) if half == 0 else (4, 5, 6, 7)
                for jt in range(njt):
                    pt = b.r("S3", jt * 256, 256, BF16)
                    for bi, dc in enumerate(dcs):
                        vt = vload[jt]
                        A("tensor", lambda e, jt=jt, dc=dc, bi=bi, vt=vt, pt=pt:
                          e.matmul(ps[bi][:, :],
                                   vt.ap[:, dc * 128:(dc + 1) * 128],
                                   pt.ap, start=(jt == 0), stop=(jt == njt - 1)),
                          reads=[vt, pt], writes=[("ps", bi)],
                          tag=f"at0_av{ic}_{half}_{jt}_{dc}")
                    if half == 0:
                        A("tensor", lambda e, jt=jt, pt=pt:
                          e.matmul(ps[4][0:1, :], onescol_b.ap[:, 0:1], pt.ap,
                                   start=(jt == 0), stop=(jt == njt - 1)),
                          reads=[onescol_b, pt], writes=[("ps", 4)],
                          tag=f"at0_dn{ic}_{jt}")
                if half == 0:
                    A("vector", lambda e: e.reciprocal(rcp.ap, ps[4][0:1, :]),
                      reads=[("ps", 4)], writes=[rcp], tag=f"at0_rc{ic}")
                    A("tensor", lambda e: e.matmul(
                        ps[4][:, :], ones1x128_f.ap, rcp.ap, start=True, stop=True),
                      reads=[ones1x128_f, rcp], writes=[("ps", 4)], tag=f"at0_bc{ic}")
                    A("vector", lambda e: e.tensor_copy(bcr.ap, ps[4][:, :]),
                      reads=[("ps", 4)], writes=[bcr], tag=f"at0_bcev{ic}")
                for bi, dc in enumerate(dcs):
                    A("vector", lambda e, bi=bi: e.tensor_tensor(
                        stg.ap, ps[bi][:, :], bcr.ap, AluOpType.mult),
                      reads=[("ps", bi), bcr], writes=[stg], tag=f"at0_nm{ic}_{half}_{bi}")
                    A("gpsimd", lambda e, dc=dc, ic=ic: e.dma_start(
                        out=H["attn0"][dc * 128:(dc + 1) * 128, ic * 512:(ic + 1) * 512],
                        in_=stg.ap),
                      reads=[stg], writes=[("attn0", ic, dc)], is_dma=True,
                      tag=f"at0_st{ic}_{half}_{dc}")
    phases.append(("attn0", ph_attn0))

    def ph_x2mlp():
        outw = b.r("S1", 12288, 4096, BF16)
        A("gpsimd", lambda e: e.dma_start(
            out=outw.ap.rearrange("p (c f) -> p c f", c=8),
            in_=H["outW"].rearrange("(c p) f -> p c f", p=128)),
          writes=[outw], is_dma=True, tag="ld_outW")
        w2sb = b.r("S2", 0, 16384, BF16)
        A("gpsimd", lambda e: e.dma_start(
            out=w2sb.ap.rearrange("p (g f) -> p g f", g=32),
            in_=H["mW2"].rearrange("(g p) f -> p g f", p=128)),
          writes=[w2sb], is_dma=True, tag="ld_mW2")
        for nm, c0, n in [("outb8", 808, 8), ("ln2g8", 816, 8), ("ln2b8", 824, 8),
                          ("mb2c", 832, 8), ("mb1c", 776, 32)]:
            rg = b.r("SM", c0, n)
            globals()["_sm_" + nm] = rg
            b.load(rg, H[nm][:, :], tag=f"ld_{nm}")
        outb8 = b.r("SM", 808, 8); ln2g8 = b.r("SM", 816, 8)
        ln2b8 = b.r("SM", 824, 8); mb2c = b.r("SM", 832, 8); mb1c = b.r("SM", 776, 32)
        gbuf = b.r("S1", 0, 8192, BF16)
        attnsl = b.r("S1", 8192, 2048, BF16)
        hbuf = b.r("S1", 10240, 2048, BF16)
        x2sl = b.r("S3", 0, 4096)
        xnsl = b.r("S3", 4096, 4096)
        x1p = b.r("S4", 0, 4096)
        w1db = [b.r("S4", 4096, 512, BF16), b.r("S4", 4608, 512, BF16)]
        sq = b.r("S4", 5120, 512)
        rw = [Rg(b.slabs["S4"][0][0:1, 5632 + i * 512:6144 + i * 512],
                 b.r("S4", 5632 + i * 512, 512).keys) for i in range(4)]
        bc_r2 = b.r("S4", 7680, 512)
        bc_mr2 = Rg(b.slabs["S4"][0][:, 5632:6144], b.r("S4", 5632, 512).keys)
        epsap, epskey = cst(EPS)

        for ic in range(8):
            A("sync", lambda e, ic=ic: e.dma_start(
                out=attnsl.ap.rearrange("p (c n) -> p c n", c=8),
                in_=H["attn0"][:, ic * 512:(ic + 1) * 512].rearrange(
                    "(c p) n -> p c n", p=128)),
              reads=[("attn0", ic, dc) for dc in range(8)], writes=[attnsl],
              is_dma=True, tag=f"x2_lda{ic}")
            A("sync", lambda e, ic=ic: e.dma_start(
                out=xnsl.ap.rearrange("p (c n) -> p c n", c=8),
                in_=H["xnf"][:, ic * 512:(ic + 1) * 512].rearrange(
                    "(c p) n -> p c n", p=128)),
              reads=[("xnf", c) for c in range(8)], writes=[xnsl],
              is_dma=True, tag=f"x2_ldxn{ic}")
            for fc in range(8):
                bk = fc % 2
                for kc in range(8):
                    A("tensor", lambda e, fc=fc, kc=kc, bk=bk: e.matmul(
                        ps[bk][:, :],
                        outw.ap[:, kc * 1024 + fc * 128: kc * 1024 + fc * 128 + 128],
                        attnsl.ap[:, kc * 512:(kc + 1) * 512],
                        start=(kc == 0), stop=(kc == 7)),
                      reads=[outw, attnsl], writes=[("ps", bk)], tag=f"x2_mm{ic}_{fc}_{kc}")
                A("vector", lambda e, fc=fc, bk=bk: e.tensor_scalar(
                    x2sl.ap[:, fc * 512:(fc + 1) * 512], ps[bk][:, :],
                    outb8.ap[:, fc:fc + 1], None, AluOpType.add),
                  reads=[("ps", bk), outb8], writes=[x2sl], tag=f"x2_ev{ic}_{fc}")
                A("vector", lambda e, fc=fc: e.tensor_tensor(
                    x2sl.ap[:, fc * 512:(fc + 1) * 512],
                    x2sl.ap[:, fc * 512:(fc + 1) * 512],
                    xnsl.ap[:, fc * 512:(fc + 1) * 512], AluOpType.add),
                  reads=[x2sl, xnsl], writes=[x2sl], tag=f"x2_addxn{ic}_{fc}")
            for fc in range(8):
                A("tensor", lambda e, fc=fc: e.matmul(
                    ps[2][0:1, :], onescol_f.ap, x2sl.ap[:, fc * 512:(fc + 1) * 512],
                    start=(fc == 0), stop=(fc == 7)),
                  reads=[x2sl, onescol_f], writes=[("ps", 2)], tag=f"l2s1_{ic}_{fc}")
                A("scalar", lambda e, fc=fc: e.activation(
                    out=sq.ap, in_=x2sl.ap[:, fc * 512:(fc + 1) * 512], func=AF.Square),
                  reads=[x2sl], writes=[sq], tag=f"l2sq_{ic}_{fc}")
                A("tensor", lambda e, fc=fc: e.matmul(
                    ps[2][32:33, :], onescol_f.ap, sq.ap,
                    start=(fc == 0), stop=(fc == 7)),
                  reads=[sq, onescol_f], writes=[("ps", 2)], tag=f"l2s2_{ic}_{fc}")
            A("vector", lambda e: e.tensor_scalar_mul(rw[1].ap, ps[2][0:1, :], 1.0 / D),
              reads=[("ps", 2)], writes=[rw[1]], tag=f"l2m_{ic}")
            A("vector", lambda e: e.tensor_scalar_mul(rw[2].ap, ps[2][32:33, :], 1.0 / D),
              reads=[("ps", 2)], writes=[rw[2]], tag=f"l2ss_{ic}")
            A("vector", lambda e: e.tensor_tensor(rw[3].ap, rw[1].ap, rw[1].ap, AluOpType.mult),
              reads=[rw[1]], writes=[rw[3]], tag=f"l2m2_{ic}")
            A("vector", lambda e: e.tensor_tensor(rw[2].ap, rw[2].ap, rw[3].ap, AluOpType.subtract),
              reads=[rw[2], rw[3]], writes=[rw[2]], tag=f"l2v_{ic}")
            A("scalar", lambda e: e.activation(out=rw[3].ap, in_=rw[2].ap, func=AF.Sqrt,
                                               bias=epsap[0:1, 0:1]),
              reads=[rw[2], epskey], writes=[rw[3]], tag=f"l2sq2_{ic}")
            A("vector", lambda e: e.reciprocal(rw[2].ap, rw[3].ap),
              reads=[rw[3]], writes=[rw[2]], tag=f"l2r_{ic}")
            A("vector", lambda e: e.tensor_tensor(rw[3].ap, rw[1].ap, rw[2].ap, AluOpType.mult),
              reads=[rw[1], rw[2]], writes=[rw[3]], tag=f"l2mr_{ic}")
            A("tensor", lambda e: e.matmul(ps[2][:, :], ones1x128_f.ap, rw[2].ap,
                                           start=True, stop=True),
              reads=[ones1x128_f, rw[2]], writes=[("ps", 2)], tag=f"l2bcr_{ic}")
            A("vector", lambda e: e.tensor_copy(bc_r2.ap, ps[2][:, :]),
              reads=[("ps", 2)], writes=[bc_r2], tag=f"l2bcrev_{ic}")
            A("tensor", lambda e: e.matmul(ps[2][:, :], ones1x128_f.ap, rw[3].ap,
                                           start=True, stop=True),
              reads=[ones1x128_f, rw[3]], writes=[("ps", 2)], tag=f"l2bcm_{ic}")
            A("vector", lambda e: e.tensor_copy(bc_mr2.ap, ps[2][:, :]),
              reads=[("ps", 2)], writes=[bc_mr2], tag=f"l2bcmev_{ic}")
            for fc in range(8):
                A("vector", lambda e, fc=fc: e.tensor_tensor(
                    sq.ap, x2sl.ap[:, fc * 512:(fc + 1) * 512], bc_r2.ap, AluOpType.mult),
                  reads=[x2sl, bc_r2], writes=[sq], tag=f"h1_{ic}_{fc}")
                A("vector", lambda e: e.tensor_tensor(
                    sq.ap, sq.ap, bc_mr2.ap, AluOpType.subtract),
                  reads=[sq, bc_mr2], writes=[sq], tag=f"h2_{ic}_{fc}")
                A("vector", lambda e, fc=fc: e.tensor_scalar(
                    hbuf.ap[:, fc * 512:(fc + 1) * 512], sq.ap,
                    ln2g8.ap[:, fc:fc + 1], ln2b8.ap[:, fc:fc + 1],
                    AluOpType.mult, AluOpType.add),
                  reads=[sq, ln2g8, ln2b8], writes=[hbuf], tag=f"h3_{ic}_{fc}")
            for fg in range(32):
                wdb = w1db[fg % 2]
                if ic == 0:
                    A("gpsimd", lambda e, fg=fg, wdb=wdb: e.dma_start(
                        out=wdb.ap.rearrange("p (c f) -> p c f", c=8),
                        in_=H["mW1"][:, fg * 128:(fg + 1) * 128].rearrange(
                            "(c p) f -> p c f", p=128)),
                      writes=[wdb], is_dma=True, tag=f"m1ld_{ic}_{fg}")
                    A("sync", lambda e, fg=fg, wdb=wdb: e.dma_start(
                        out=H["mW1b"][:, fg * 1024:(fg + 1) * 1024], in_=wdb.ap),
                      reads=[wdb], writes=[("mW1b", fg)], is_dma=True,
                      tag=f"m1st_{fg}")
                else:
                    A("sync" if fg % 2 == 0 else "scalar",
                      lambda e, fg=fg, wdb=wdb: e.dma_start(
                        out=wdb.ap, in_=H["mW1b"][:, fg * 1024:(fg + 1) * 1024]),
                      reads=[("mW1b", fg)], writes=[wdb], is_dma=True,
                      tag=f"m1ld_{ic}_{fg}")
                bank = 3 + (fg % 2)
                for kc in range(8):
                    A("tensor", lambda e, fg=fg, kc=kc, wdb=wdb, bank=bank: e.matmul(
                        ps[bank][:, :], wdb.ap[:, kc * 128:(kc + 1) * 128],
                        hbuf.ap[:, kc * 512:(kc + 1) * 512],
                        start=(kc == 0), stop=(kc == 7)),
                      reads=[wdb, hbuf], writes=[("ps", bank)], tag=f"m1mm_{ic}_{fg}_{kc}")
                A("scalar", lambda e, fg=fg, bank=bank: e.activation(
                    out=gbuf.ap[:, fg * 512:(fg + 1) * 512], in_=ps[bank][:, :],
                    func=AF.Gelu, bias=mb1c.ap[:, fg:fg + 1]),
                  reads=[("ps", bank), mb1c], writes=[gbuf], tag=f"m1g_{ic}_{fg}")
            for fc in range(8):
                bank = 5 + (fc % 3)
                for gc in range(32):
                    A("tensor", lambda e, fc=fc, gc=gc, bank=bank: e.matmul(
                        ps[bank][:, :],
                        w2sb.ap[:, gc * 1024 + fc * 128: gc * 1024 + fc * 128 + 128],
                        gbuf.ap[:, gc * 512:(gc + 1) * 512],
                        start=(gc == 0), stop=(gc == 31)),
                      reads=[w2sb, gbuf], writes=[("ps", bank)], tag=f"m2mm_{ic}_{fc}_{gc}")
                A("vector", lambda e, fc=fc, bank=bank: e.tensor_scalar(
                    x1p.ap[:, fc * 512:(fc + 1) * 512], ps[bank][:, :],
                    mb2c.ap[:, fc:fc + 1], None, AluOpType.add),
                  reads=[("ps", bank), mb2c], writes=[x1p], tag=f"m2ev_{ic}_{fc}")
                A("vector", lambda e, fc=fc: e.tensor_tensor(
                    x1p.ap[:, fc * 512:(fc + 1) * 512],
                    x1p.ap[:, fc * 512:(fc + 1) * 512],
                    x2sl.ap[:, fc * 512:(fc + 1) * 512], AluOpType.add),
                  reads=[x1p, x2sl], writes=[x1p], tag=f"m2x2_{ic}_{fc}")
            A("gpsimd", lambda e, ic=ic: e.dma_start(
                out=H["x1p"][:, ic * 512:(ic + 1) * 512].rearrange(
                    "(c p) n -> p c n", p=128),
                in_=x1p.ap.rearrange("p (c n) -> p c n", c=8)),
              reads=[x1p], writes=[("x1p", ic, fc2) for fc2 in range(8)],
              is_dma=True, tag=f"m2st_{ic}")
    phases.append(("x2mlp", ph_x2mlp))

    def ph_ffn0():
        fw2 = b.r("S2", 0, 16384, BF16)
        A("gpsimd", lambda e: e.dma_start(
            out=fw2.ap.rearrange("p (g f) -> p g f", g=32),
            in_=H["fW2"].rearrange("(g p) f -> p g f", p=128)),
          writes=[fw2], is_dma=True, tag="ld_fW2")
        wrt = b.r("SM", 840, 32, BF16)
        A("gpsimd", lambda e: e.dma_start(
            out=wrt.ap.rearrange("p (c f) -> p c f", c=8),
            in_=H["wrT"].rearrange("(c p) f -> p c f", p=128)),
          writes=[wrt], is_dma=True, tag="ld_wrT")
        hh = b.r("S1", 0, 8192, BF16)
        xnbf = b.r("S1", 8192, 2048, BF16)
        x1psl = b.r("S3", 0, 4096)
        xnsl = b.r("S3", 4096, 4096)
        ybuf = b.r("S4", 0, 4096)
        bcs = b.r("S4", 4096, 512)
        sqB = b.r("S4", 4608, 512)
        sel4 = Rg(b.slabs["S4"][0][0:4, 5120:5632], b.r("S4", 5120, 512).keys)
        b.load(sel4, H["sel8"][0:4, 0:512], tag="ld_sel4")
        sm8 = b.r("S4", 6144, 8)
        nmx = b.r("S4", 6152, 1)
        rcq = b.r("S4", 6154, 1)
        smax = b.r("S4", 6156, 1)
        ssum = b.r("S4", 6158, 1)
        p7c = b.r("S4", 6160, 4)
        fb1c = b.r("S4", 6400, 32)
        fb2c = b.r("S4", 6432, 8)
        tr4 = Rg(b.slabs["S4"][0][0:4, 6464:6592], b.r("S4", 6464, 128).keys)
        p7sl = Rg(b.slabs["S4"][0][0:1, 6592:7104], b.r("S4", 6592, 512).keys)
        w1db = [b.r("S4", 7168, 512, BF16), b.r("S4", 7680, 512, BF16)]
        bc_p7 = None
        b.load(fb1c, H["fb1c"][:, :], tag="ld_fb1c")
        b.load(fb2c, H["fb2c"][:, :], tag="ld_fb2c")

        for ic in range(8):
            A("sync", lambda e, ic=ic: e.dma_start(
                out=xnbf.ap.rearrange("p (c n) -> p c n", c=8),
                in_=H["xnfb"][:, ic * 512:(ic + 1) * 512].rearrange(
                    "(c p) n -> p c n", p=128)),
              reads=[("xnfb", c) for c in range(8)], writes=[xnbf],
              is_dma=True, tag=f"f_ldxnb{ic}")
            for t4 in range(4):
                for kc in range(8):
                    A("tensor", lambda e, t4=t4, kc=kc: e.matmul(
                        ps[0][:, 0:8],
                        xnbf.ap[:, kc * 512 + t4 * 128: kc * 512 + t4 * 128 + 128],
                        wrt.ap[:, kc * 8:(kc + 1) * 8],
                        start=(kc == 0), stop=(kc == 7)),
                      reads=[xnbf, wrt], writes=[("ps", 0)], tag=f"rt_{ic}_{t4}_{kc}")
                A("vector", lambda e: e.reduce_max(smax.ap, ps[0][:, 0:8], axis=AX.X),
                  reads=[("ps", 0)], writes=[smax], tag=f"rtm_{ic}_{t4}")
                A("vector", lambda e: e.tensor_scalar_mul(nmx.ap, smax.ap, -1.0),
                  reads=[smax], writes=[nmx], tag=f"rtn_{ic}_{t4}")
                A("scalar", lambda e: e.activation(
                    out=sm8.ap, in_=ps[0][:, 0:8], func=AF.Exp, bias=nmx.ap[:, 0:1]),
                  reads=[("ps", 0), nmx], writes=[sm8], tag=f"rte_{ic}_{t4}")
                A("vector", lambda e: e.reduce_sum(ssum.ap, sm8.ap, axis=AX.X),
                  reads=[sm8], writes=[ssum], tag=f"rts_{ic}_{t4}")
                A("vector", lambda e: e.reciprocal(rcq.ap, ssum.ap),
                  reads=[ssum], writes=[rcq], tag=f"rtr_{ic}_{t4}")
                A("vector", lambda e, t4=t4: e.tensor_tensor(
                    p7c.ap[:, t4:t4 + 1], sm8.ap[:, 7:8], rcq.ap, AluOpType.mult),
                  reads=[sm8, rcq], writes=[p7c], tag=f"rtp_{ic}_{t4}")
            A("tensor", lambda e: e.transpose(ps[1][0:4, 0:128], p7c.ap, identf_sb.ap),
              reads=[p7c, identf_sb], writes=[("ps", 1)], tag=f"p7t_{ic}")
            A("vector", lambda e: e.tensor_copy(tr4.ap, ps[1][0:4, 0:128]),
              reads=[("ps", 1)], writes=[tr4], tag=f"p7te_{ic}")
            bcp7 = bcs
            for t4 in range(4):
                A("tensor", lambda e, t4=t4: e.matmul(
                    ps[2][0:128, 0:128],
                    sel4.ap[0:4, t4 * 128:(t4 + 1) * 128],
                    tr4.ap[0:4, 0:128], start=True, stop=True),
                  reads=[sel4, tr4], writes=[("ps", 2)],
                  tag=f"p7bc_{ic}_{t4}")
                A("vector", lambda e, t4=t4: e.tensor_copy(
                    bcp7.ap[:, t4 * 128:(t4 + 1) * 128],
                    ps[2][0:128, 0:128]),
                  reads=[("ps", 2)], writes=[bcp7],
                  tag=f"p7bce_{ic}_{t4}")
            for fg in range(32):
                wdb = w1db[fg % 2]
                if ic == 0:
                    A("gpsimd", lambda e, fg=fg, wdb=wdb: e.dma_start(
                        out=wdb.ap.rearrange("p (c f) -> p c f", c=8),
                        in_=H["fW1"][:, fg * 128:(fg + 1) * 128].rearrange(
                            "(c p) f -> p c f", p=128)),
                      writes=[wdb], is_dma=True, tag=f"f1ld_{ic}_{fg}")
                    A("sync", lambda e, fg=fg, wdb=wdb: e.dma_start(
                        out=H["fW1b"][:, fg * 1024:(fg + 1) * 1024], in_=wdb.ap),
                      reads=[wdb], writes=[("fW1b", fg)], is_dma=True,
                      tag=f"f1st_{fg}")
                else:
                    A("sync" if fg % 2 == 0 else "scalar",
                      lambda e, fg=fg, wdb=wdb: e.dma_start(
                        out=wdb.ap, in_=H["fW1b"][:, fg * 1024:(fg + 1) * 1024]),
                      reads=[("fW1b", fg)], writes=[wdb], is_dma=True,
                      tag=f"f1ld_{ic}_{fg}")
                bank = 3 + (fg % 2)
                for kc in range(8):
                    A("tensor", lambda e, fg=fg, kc=kc, wdb=wdb, bank=bank: e.matmul(
                        ps[bank][:, :], wdb.ap[:, kc * 128:(kc + 1) * 128],
                        xnbf.ap[:, kc * 512:(kc + 1) * 512],
                        start=(kc == 0), stop=(kc == 7)),
                      reads=[wdb, xnbf], writes=[("ps", bank)], tag=f"f1mm_{ic}_{fg}_{kc}")
                A("scalar", lambda e, fg=fg, bank=bank: e.activation(
                    out=hh.ap[:, fg * 512:(fg + 1) * 512], in_=ps[bank][:, :],
                    func=AF.Gelu, bias=fb1c.ap[:, fg:fg + 1]),
                  reads=[("ps", bank), fb1c], writes=[hh], tag=f"f1g_{ic}_{fg}")
            for fc in range(8):
                bank = 5 + (fc % 3)
                for gc in range(32):
                    A("tensor", lambda e, fc=fc, gc=gc, bank=bank: e.matmul(
                        ps[bank][:, :],
                        fw2.ap[:, gc * 1024 + fc * 128: gc * 1024 + fc * 128 + 128],
                        hh.ap[:, gc * 512:(gc + 1) * 512],
                        start=(gc == 0), stop=(gc == 31)),
                      reads=[fw2, hh], writes=[("ps", bank)], tag=f"f2mm_{ic}_{fc}_{gc}")
                A("vector", lambda e, fc=fc, bank=bank: e.tensor_scalar(
                    ybuf.ap[:, fc * 512:(fc + 1) * 512], ps[bank][:, :],
                    fb2c.ap[:, fc:fc + 1], None, AluOpType.add),
                  reads=[("ps", bank), fb2c], writes=[ybuf], tag=f"f2ev_{ic}_{fc}")
                A("vector", lambda e, fc=fc: e.tensor_tensor(
                    ybuf.ap[:, fc * 512:(fc + 1) * 512],
                    ybuf.ap[:, fc * 512:(fc + 1) * 512], bcp7.ap, AluOpType.mult),
                  reads=[ybuf, bcp7], writes=[ybuf], tag=f"f2p7_{ic}_{fc}")
            A("sync", lambda e, ic=ic: e.dma_start(
                out=x1psl.ap.rearrange("p (c n) -> p c n", c=8),
                in_=H["x1p"][:, ic * 512:(ic + 1) * 512].rearrange(
                    "(c p) n -> p c n", p=128)),
              reads=[("x1p", ic, fc2) for fc2 in range(8)], writes=[x1psl],
              is_dma=True, tag=f"as_ldx1p{ic}")
            A("sync", lambda e, ic=ic: e.dma_start(
                out=xnsl.ap.rearrange("p (c n) -> p c n", c=8),
                in_=H["xT"][:, ic * 512:(ic + 1) * 512].rearrange(
                    "(c p) n -> p c n", p=128)),
              writes=[xnsl], is_dma=True, tag=f"as_ldx{ic}")
            A("vector", lambda e: e.tensor_tensor(
                ybuf.ap, ybuf.ap, xnsl.ap, AluOpType.add),
              reads=[ybuf, xnsl], writes=[ybuf], tag=f"as_a1{ic}")
            A("vector", lambda e: e.tensor_tensor(
                ybuf.ap, ybuf.ap, x1psl.ap, AluOpType.add),
              reads=[ybuf, x1psl], writes=[ybuf], tag=f"as_a2{ic}")
            A("gpsimd", lambda e, ic=ic: e.dma_start(
                out=H["x1s"][:, ic * 512:(ic + 1) * 512].rearrange(
                    "(c p) n -> p c n", p=128),
                in_=ybuf.ap.rearrange("p (c n) -> p c n", c=8)),
              reads=[ybuf], writes=[("x1s", ic, fc2) for fc2 in range(8)],
              is_dma=True, tag=f"as_st{ic}")
    phases.append(("ffn0", ph_ffn0))

    phases.append(("ln1", lambda: ph_ln(1)))
    phases.append(("qkv1", lambda: ph_qkv(1)))

    def ph_attn1():
        for c in range(8):
            blk = b.r("S1", c * 2048, 2048, BF16)
            A("sync",
              lambda e, c=c, blk=blk: e.dma_start_transpose(
                out=blk.ap, in_=H["ktm1"][:, c * 128:(c + 1) * 128]),
              reads=[("ktm1", tt) for tt in range(32)], writes=[blk],
              is_dma=True, tag=f"a1_ldk{c}")
        vstages = [b.r("S4", 4096, 512, BF16), b.r("S4", 4608, 512, BF16)]
        for jt in range(32):
            vstage = vstages[jt % 2]
            A("sync" if jt % 2 == 0 else "gpsimd",
              lambda e, jt=jt, vstage=vstage: e.dma_start(
                out=vstage.ap, in_=H["vtm1"][jt * 128:(jt + 1) * 128, :]),
              reads=[("vtm1", jt)], writes=[vstage], is_dma=True, tag=f"a1_ldv{jt}")
            dst = b.r("S2", jt * 520, 520, BF16)
            A("vector", lambda e, dst=dst, vstage=vstage: e.tensor_copy(
                dst.ap.rearrange("p (h c) -> p h c", h=16)[:, :, 0:64],
                vstage.ap.rearrange("p (h c) -> p h c", h=16)),
              reads=[vstage], writes=[dst], tag=f"a1_v{jt}")
            A("vector", lambda e, dst=dst: e.memset(
                dst.ap.rearrange("p (h c) -> p h c", h=16)[:, :, 64:65], 1.0),
              reads=[], writes=[dst], tag=f"a1_vo{jt}")
        qsl = [b.r("S4", 0, 2048, BF16), b.r("S4", 2048, 2048, BF16)]
        stg = [b.r("S4", 5120, 128, BF16), b.r("S4", 5632, 128, BF16)]
        rc4 = [b.r("S4", 5248, 4), b.r("S4", 5760, 4)]
        for ic in range(8):
            qb = qsl[ic % 2]
            for c in range(8):
                A("sync",
                  lambda e, ic=ic, qb=qb, c=c: e.dma_start_transpose(
                    out=qb.ap[:, c * 512:(c + 1) * 512],
                    in_=H["qtm1"][ic * 512:(ic + 1) * 512, c * 128:(c + 1) * 128]),
                  reads=[("qtm1", tt) for tt in range(4 * ic, 4 * ic + 4)], writes=[qb],
                  is_dma=True, tag=f"a1_ldq{ic}_{c}")
            for h in range(16):
                blk = b.r("S1", (h // 2) * 2048, 2048, BF16)
                po = 64 * (h % 2)
                avb = h % 2
                for jt in range(32):
                    jp, half = divmod(jt, 2)
                    D = 1 + (jp % 3)
                    bank = 2 * D + half
                    A("tensor", lambda e, jt=jt, h=h, po=po, blk=blk, bank=bank, ic=ic:
                      e.matmul(ps[bank][:, :],
                               blk.ap[po:po + 64, jt * 128:(jt + 1) * 128],
                               qsl[ic % 2].ap[po:po + 64, (h // 2) * 512:(h // 2) * 512 + 512],
                               start=True, stop=True),
                      reads=[blk, qsl[ic % 2]], writes=[("ps", bank)],
                      tag=f"a1s_{ic}_{h}_{jt}")
                    if half == 1:
                        ptp = b.r("S3", (jt - 1) * 256, 512, BF16)
                        A("scalar", lambda e, D=D, ptp=ptp: e.activation(
                            out=ptp.ap, in_=b.psd[D][0:128, 0:1024],
                            func=AF.Exp, scale=1.0 / 8.0),
                          reads=[("ps", 2 * D), ("ps", 2 * D + 1)], writes=[ptp],
                          tag=f"a1e_{ic}_{h}_{jt}")
                for jt in range(32):
                    pt = b.r("S3", jt * 256, 256, BF16)
                    vt = b.r("S2", jt * 520, 520, BF16)
                    for qs in range(4):
                        A("tensor", lambda e, jt=jt, h=h, qs=qs, vt=vt, pt=pt, avb=avb:
                          e.matmul(ps[avb][:, qs * 65:qs * 65 + 65],
                                   pt.ap[:, qs * 128:(qs + 1) * 128],
                                   vt.ap[:, h * 65:h * 65 + 65],
                                   start=(jt == 0), stop=(jt == 31)),
                          reads=[vt, pt], writes=[("ps", avb)],
                          tag=f"a1v_{ic}_{h}_{jt}_{qs}")
                sg = stg[h % 2]
                rc = rc4[h % 2]
                for qs in range(4):
                    A("vector", lambda e, qs=qs, avb=avb, rc=rc: e.reciprocal(
                        rc.ap[:, qs:qs + 1], ps[avb][:, qs * 65 + 64:qs * 65 + 65]),
                      reads=[("ps", avb)], writes=[rc], tag=f"a1r_{ic}_{h}_{qs}")
                    A("vector", lambda e, qs=qs, avb=avb, rc=rc, sg=sg: e.tensor_scalar(
                        sg.ap[:, qs * 64:(qs + 1) * 64],
                        ps[avb][:, qs * 65:qs * 65 + 64],
                        rc.ap[:, qs:qs + 1], None, AluOpType.mult),
                      reads=[("ps", avb), rc], writes=[sg], tag=f"a1n_{ic}_{h}_{qs}")
                A("gpsimd", lambda e, h=h, ic=ic, sg=sg: e.dma_start(
                    out=H["attn1t"][ic * 512:(ic + 1) * 512,
                                    h * 64:(h + 1) * 64].rearrange(
                        "(q p) d -> p q d", p=128),
                    in_=sg.ap.rearrange("p (q d) -> p q d", q=4)),
                  reads=[sg], writes=[("attn1t", ic, h)], is_dma=True,
                  tag=f"a1st_{ic}_{h}")
    phases.append(("attn1", ph_attn1))

    def ph_moe():
        wusb = b.r("S2", 0, 4096, BF16)
        A("gpsimd", lambda e: e.dma_start(
            out=wusb.ap.rearrange("p (c f) -> p c f", c=8),
            in_=H["wuT"].rearrange("(c p) f -> p c f", p=128)),
          writes=[wusb], is_dma=True, tag="ld_wuT")
        g1t = b.r("SM", 904, 32, BF16)
        A("gpsimd", lambda e: e.dma_start(
            out=g1t.ap.rearrange("p (c f) -> p c f", c=8),
            in_=H["g1T"].rearrange("(c p) f -> p c f", p=128)),
          writes=[g1t], is_dma=True, tag="ld_g1T")
        bu8 = b.r("SM", 936, 8)
        b.load(bu8, H["bu8"][:, :], tag="ld_bu8")
        ew = [b.r("S1", j * 4096, 4096, BF16) for j in range(4)]
        zbf = b.r("S2", 4096, 2048, BF16)
        xms = [b.r("S2", 6144, 2048, BF16), b.r("S2", 8192, 2048, BF16)]
        attnsl = b.r("S2", 10240, 2048, BF16)
        bcme = [b.r("S2", 12288 + j * 256, 256, BF16) for j in range(4)]
        acc = b.r("S3", 0, 4096)
        znsl = b.r("S3", 4096, 4096)
        x1sl = b.r("S4", 0, 4096)
        ysc = b.r("S4", 6144, 512)
        msk8 = b.r("S4", 6656, 8)
        mmx = b.r("S4", 6664, 1)
        tr8 = Rg(b.slabs["S4"][0][0:8, 6672:6800], b.r("S4", 6672, 128).keys)
        mrow1 = Rg(b.slabs["S4"][0][0:1, 6800:7312], b.r("S4", 6800, 512).keys)

        def ld_experts(rnd):
            for j in range(4):
                e_ = rnd * 4 + j
                A("gpsimd",
                  lambda e, e_=e_, j=j: e.dma_start(
                    out=ew[j].ap.rearrange("p (c f) -> p c f", c=8),
                    in_=H["ewT"][e_].rearrange("(c p) f -> p c f", p=128)),
                  writes=[ew[j]], is_dma=True, tag=f"ew_{rnd}_{j}")

        def ld_zbf(ic, rnd):
            A("sync", lambda e, ic=ic: e.dma_start(
                out=zbf.ap.rearrange("p (c n) -> p c n", c=8),
                in_=H["zbfh"][:, ic * 512:(ic + 1) * 512].rearrange(
                    "(c p) n -> p c n", p=128)),
              reads=[("zbfh", c) for c in range(8)], writes=[zbf],
              is_dma=True, tag=f"mo_ldzb{rnd}_{ic}")

        sel8s = Rg(b.slabs["S4"][0][0:8, 4096:5120], b.r("S4", 4096, 1024).keys)
        b.load(sel8s, H["sel8"][:, :], tag="ld_sel8")
        msk2 = [msk8, b.r("S4", 5120, 8)]
        mmx2 = [mmx, b.r("S4", 5136, 1)]
        tr82 = [tr8, Rg(b.slabs["S4"][0][0:8, 5184:5312], b.r("S4", 5184, 128).keys)]

        def mask_chain(ic, rnd):
            for t4 in range(4):
                mk = msk2[t4 % 2]
                mx = mmx2[t4 % 2]
                t8 = tr82[t4 % 2]
                for kc in range(8):
                    A("tensor", lambda e, t4=t4, kc=kc: e.matmul(
                        ps[0][:, 0:8],
                        zbf.ap[:, kc * 512 + t4 * 128: kc * 512 + t4 * 128 + 128],
                        g1t.ap[:, kc * 8:(kc + 1) * 8],
                        start=(kc == 0), stop=(kc == 7)),
                      reads=[zbf, g1t], writes=[("ps", 0)], tag=f"mg{rnd}_{ic}_{t4}_{kc}")
                A("vector", lambda e, mx=mx: e.reduce_max(mx.ap, ps[0][:, 0:8], axis=AX.X),
                  reads=[("ps", 0)], writes=[mx], tag=f"mgm{rnd}_{ic}_{t4}")
                A("vector", lambda e, mk=mk, mx=mx: e.tensor_scalar(
                    mk.ap, ps[0][:, 0:8], mx.ap[:, 0:1], None, AluOpType.is_ge),
                  reads=[("ps", 0), mx], writes=[mk], tag=f"mgk{rnd}_{ic}_{t4}")
                A("tensor", lambda e, mk=mk: e.transpose(
                    ps[1][0:8, 0:128], mk.ap, identf_sb.ap),
                  reads=[mk, identf_sb], writes=[("ps", 1)], tag=f"mgt{rnd}_{ic}_{t4}")
                A("vector", lambda e, t8=t8: e.tensor_copy(t8.ap, ps[1][0:8, 0:128]),
                  reads=[("ps", 1)], writes=[t8], tag=f"mgte{rnd}_{ic}_{t4}")
                for j in range(4):
                    e_ = rnd * 4 + j
                    bank = 2 + (j % 2)
                    A("tensor", lambda e, e_=e_, t8=t8, bank=bank: e.matmul(
                        ps[bank][0:128, 0:128],
                        sel8s.ap[0:8, e_ * 128:(e_ + 1) * 128],
                        t8.ap[0:8, 0:128], start=True, stop=True),
                      reads=[sel8s, t8], writes=[("ps", bank)],
                      tag=f"mb{rnd}_{ic}_{t4}_{j}")
                    A("vector", lambda e, j=j, t4=t4, bank=bank: e.tensor_copy(
                        bcme[j].ap[:, t4 * 128:(t4 + 1) * 128],
                        ps[bank][0:128, 0:128]),
                      reads=[("ps", bank)], writes=[bcme[j]],
                      tag=f"mbe{rnd}_{ic}_{t4}_{j}")

        def expert_mms(ic, rnd):
            for j in range(4):
                xm = xms[j % 2]
                for c in range(8):
                    A("vector", lambda e, j=j, c=c, xm=xm: e.tensor_tensor(
                        xm.ap[:, c * 512:(c + 1) * 512],
                        zbf.ap[:, c * 512:(c + 1) * 512], bcme[j].ap, AluOpType.mult),
                      reads=[zbf, bcme[j]], writes=[xm], tag=f"xm{rnd}_{ic}_{j}_{c}")
                for fc in range(8):
                    for kc in range(8):
                        A("tensor", lambda e, j=j, fc=fc, kc=kc, xm=xm: e.matmul(
                            ps[fc][:, :],
                            ew[j].ap[:, kc * 1024 + fc * 128: kc * 1024 + fc * 128 + 128],
                            xm.ap[:, kc * 512:(kc + 1) * 512],
                            start=(j == 0 and kc == 0), stop=(j == 3 and kc == 7)),
                          reads=[ew[j], xm], writes=[("ps", fc)],
                          tag=f"ex{rnd}_{ic}_{j}_{fc}_{kc}")

        ld_experts(0)
        for ic in range(8):
            ld_zbf(ic, 0)
            for kc in range(8):
                A("sync", lambda e, ic=ic, kc=kc: e.dma_start_transpose(
                    out=attnsl.ap[:, kc * 512:(kc + 1) * 512],
                    in_=H["attn1t"][ic * 512:(ic + 1) * 512,
                                    kc * 128:(kc + 1) * 128]),
                  reads=[("attn1t", ic, 2 * kc), ("attn1t", ic, 2 * kc + 1)],
                  writes=[attnsl], is_dma=True, tag=f"mo_lda{ic}_{kc}")
            A("gpsimd", lambda e, ic=ic: e.dma_start(
                out=znsl.ap.rearrange("p (c n) -> p c n", c=8),
                in_=H["znf"][:, ic * 512:(ic + 1) * 512].rearrange(
                    "(c p) n -> p c n", p=128)),
              reads=[("znf", c) for c in range(8)], writes=[znsl],
              is_dma=True, tag=f"mo_ldz{ic}")
            A("gpsimd", lambda e, ic=ic: e.dma_start(
                out=x1sl.ap.rearrange("p (c n) -> p c n", c=8),
                in_=H["x1s"][:, ic * 512:(ic + 1) * 512].rearrange(
                    "(c p) n -> p c n", p=128)),
              reads=[("x1s", ic2, c) for ic2 in (ic,) for c in range(8)],
              writes=[x1sl], is_dma=True, tag=f"mo_ldx1{ic}")
            mask_chain(ic, 0)
            for fc in range(8):
                for kc in range(8):
                    A("tensor", lambda e, fc=fc, kc=kc: e.matmul(
                        ps[fc][:, :],
                        wusb.ap[:, kc * 1024 + fc * 128: kc * 1024 + fc * 128 + 128],
                        attnsl.ap[:, kc * 512:(kc + 1) * 512],
                        start=(kc == 0), stop=(kc == 7)),
                      reads=[wusb, attnsl], writes=[("ps", fc)], tag=f"wu_{ic}_{fc}_{kc}")
            for fc in range(8):
                A("vector", lambda e, fc=fc: e.tensor_scalar(
                    acc.ap[:, fc * 512:(fc + 1) * 512], ps[fc][:, :],
                    bu8.ap[:, fc:fc + 1], None, AluOpType.add),
                  reads=[("ps", fc), bu8], writes=[acc], tag=f"wue_{ic}_{fc}")
                A("vector", lambda e, fc=fc: e.tensor_tensor(
                    acc.ap[:, fc * 512:(fc + 1) * 512],
                    acc.ap[:, fc * 512:(fc + 1) * 512],
                    x1sl.ap[:, fc * 512:(fc + 1) * 512], AluOpType.add),
                  reads=[acc, x1sl], writes=[acc], tag=f"wux_{ic}_{fc}")
                A("vector", lambda e, fc=fc: e.tensor_tensor(
                    acc.ap[:, fc * 512:(fc + 1) * 512],
                    acc.ap[:, fc * 512:(fc + 1) * 512],
                    znsl.ap[:, fc * 512:(fc + 1) * 512], AluOpType.add),
                  reads=[acc, znsl], writes=[acc], tag=f"wuz_{ic}_{fc}")
            expert_mms(ic, 0)
            for fc in range(8):
                A("vector", lambda e, fc=fc: e.tensor_tensor(
                    acc.ap[:, fc * 512:(fc + 1) * 512], ps[fc][:, :],
                    acc.ap[:, fc * 512:(fc + 1) * 512], AluOpType.add),
                  reads=[("ps", fc), acc], writes=[acc], tag=f"ype_{ic}_{fc}")
            A("gpsimd", lambda e, ic=ic: e.dma_start(
                out=H["yp"][:, ic * 512:(ic + 1) * 512].rearrange(
                    "(c p) n -> p c n", p=128),
                in_=acc.ap.rearrange("p (c n) -> p c n", c=8)),
              reads=[acc], writes=[("yp", ic, fc2) for fc2 in range(8)],
              is_dma=True, tag=f"ypst_{ic}")
        ld_experts(1)
        ypsl = x1sl
        for ic in range(8):
            ld_zbf(ic, 1)
            mask_chain(ic, 1)
            A("sync", lambda e, ic=ic: e.dma_start(
                out=ypsl.ap.rearrange("p (c n) -> p c n", c=8),
                in_=H["yp"][:, ic * 512:(ic + 1) * 512].rearrange(
                    "(c p) n -> p c n", p=128)),
              reads=[("yp", ic, c) for c in range(8)],
              writes=[ypsl], is_dma=True, tag=f"mo_ldyp{ic}")
            expert_mms(ic, 1)
            for fc in range(8):
                A("vector", lambda e, fc=fc: e.tensor_tensor(
                    ypsl.ap[:, fc * 512:(fc + 1) * 512], ps[fc][:, :],
                    ypsl.ap[:, fc * 512:(fc + 1) * 512], AluOpType.add),
                  reads=[("ps", fc), ypsl], writes=[ypsl], tag=f"ye_{ic}_{fc}")
            A("gpsimd", lambda e, ic=ic: e.dma_start(
                out=H["y"][:, ic * 512:(ic + 1) * 512].rearrange(
                    "(c p) n -> p c n", p=128),
                in_=ypsl.ap.rearrange("p (c n) -> p c n", c=8)),
              reads=[ypsl], writes=[("y", ic, fc2) for fc2 in range(8)],
              is_dma=True, tag=f"yst_{ic}")
    phases.append(("moe", ph_moe))

    for name, fn in phases:
        fn()
        if upto == name:
            return

def make_in_maps(inputs, n_cores=8):
    g = host_prep(inputs)
    x = np.asarray(inputs["L0_attn_x"], np.float32)
    maps = []
    for b_ in range(n_cores):
        m = dict(g)
        m["xT"] = np.ascontiguousarray(x[b_].T)
        maps.append(m)
    return maps



def kernel(**inputs):
    g = host_prep(inputs)
    x = np.asarray(inputs["L0_attn_x"], np.float32)
    n_cores = 8
    maps = []
    for bi in range(n_cores):
        m = dict(g)
        m["xT"] = np.ascontiguousarray(x[bi].T)
        maps.append(m)
    b, stats = build_program(dump=(), upto=None)
    res = run_bass_kernel_spmd(b.nc, maps, core_ids=list(range(n_cores)))
    out = np.stack([np.ascontiguousarray(r["y"].T) for r in res.results], axis=0)
    return out.astype(np.float32)
